# version 14
# baseline (speedup 1.0000x reference)
import numpy as np
from contextlib import ExitStack

import concourse.bass as bass
import concourse.mybir as mybir
from concourse.bass_utils import run_bass_kernel_spmd

F32 = mybir.dt.float32
BF16 = mybir.dt.bfloat16
AF = mybir.ActivationFunctionType
ALU = mybir.AluOpType
AX = mybir.AxisListType

S, D, NH, HD = 4096, 1024, 16, 64
NP = NH // 2
E, F = 8, 2048
DC = D // 128
FC = F // 128
NQB = S // 512
NKC = S // 128
NTT = S // 128


class Sem:
    def __init__(self, h):
        self.h = h
        self.n = 0


def build_program(upto=5, taps=(), ntt=NTT, rtr=True):
    nc = bass.Bass("TRN2", target_bir_lowering=False)

    x_h = nc.dram_tensor("x", [S, D], F32, kind="ExternalInput").ap()
    wk_h = nc.dram_tensor("wk", [NH, D, HD], F32, kind="ExternalInput").ap()
    wq_h = nc.dram_tensor("wq", [NH, D, HD], F32, kind="ExternalInput").ap()
    wv_h = nc.dram_tensor("wv", [NH, D, HD], F32, kind="ExternalInput").ap()
    wo_h = nc.dram_tensor("wo", [NH, HD, D], F32, kind="ExternalInput").ap()
    bk_h = nc.dram_tensor("bk", [NH, HD], F32, kind="ExternalInput").ap()
    bq_h = nc.dram_tensor("bq", [NH, HD], F32, kind="ExternalInput").ap()
    bv_h = nc.dram_tensor("bv", [NH, HD], F32, kind="ExternalInput").ap()
    bo_h = nc.dram_tensor("bo", [D], F32, kind="ExternalInput").ap()
    wr_h = nc.dram_tensor("wr", [D, E], F32, kind="ExternalInput").ap()
    br_h = nc.dram_tensor("br", [E], F32, kind="ExternalInput").ap()
    w1_h = nc.dram_tensor("w1", [E, D, F], F32, kind="ExternalInput").ap()
    b1_h = nc.dram_tensor("b1", [E, F], F32, kind="ExternalInput").ap()
    w2_h = nc.dram_tensor("w2", [E, F, D], F32, kind="ExternalInput").ap()
    b2_h = nc.dram_tensor("b2", [E, D], F32, kind="ExternalInput").ap()
    out_h = nc.dram_tensor("out", [S, D], F32, kind="ExternalOutput").ap()
    w2c_h = nc.dram_tensor("w2c", [E, DC, 128, FC * 128], BF16, kind="Internal").ap()

    qT, qV, qS, qG, qY = [], [], [], [], []

    st = ExitStack()
    sT = Sem(st.enter_context(nc.semaphore("sT")))
    sV = Sem(st.enter_context(nc.semaphore("sV")))
    sS = Sem(st.enter_context(nc.semaphore("sS")))
    sG = Sem(st.enter_context(nc.semaphore("sG")))
    sIni = Sem(st.enter_context(nc.semaphore("sIni")))
    sX = [Sem(st.enter_context(nc.semaphore(f"sX{i}"))) for i in range(2)]
    sWq = [Sem(st.enter_context(nc.semaphore(f"sWq{i}"))) for i in range(2)]
    sW1 = [Sem(st.enter_context(nc.semaphore(f"sW1{i}"))) for i in range(2)]
    sW2 = [Sem(st.enter_context(nc.semaphore(f"sW2{i}"))) for i in range(3)]
    sWo = Sem(st.enter_context(nc.semaphore("sWo")))
    sW2w = [Sem(st.enter_context(nc.semaphore(f"sW2w{i}"))) for i in range(3)]
    sW2r = [Sem(st.enter_context(nc.semaphore(f"sW2r{i}"))) for i in range(3)]
    sB2 = Sem(st.enter_context(nc.semaphore("sB2")))
    sO = [Sem(st.enter_context(nc.semaphore(f"sO{i}"))) for i in range(2)]
    ALLSEMS = [sT, sV, sS, sG, sIni, sX[0], sX[1], sWq[0], sWq[1],
               sW1[0], sW1[1], sW2[0], sW2[1], sW2[2], sW2w[0], sW2w[1],
               sW2w[2], sW2r[0], sW2r[1], sW2r[2], sWo, sB2, sO[0], sO[1]]

    def _emit(q, fn, sem=None, k=1):
        if sem is None:
            q.append(fn)
            return None
        sem.n += k
        q.append(lambda e, f=fn, h=sem.h, k=k: f(e).then_inc(h, k))
        return sem.n

    def eT(fn, sem=None):  return _emit(qT, fn, sem)
    def eV(fn, sem=None):  return _emit(qV, fn, sem)
    def eS(fn, sem=None):  return _emit(qS, fn, sem)
    def eG(fn, sem=None, k=16):  return _emit(qG, fn, sem, k)
    def eY(fn, sem=None, k=16):  return _emit(qY, fn, sem, k)

    def w(q, sem, v):
        if v and v > 0:
            q.append(lambda e, h=sem.h, v=v: e.wait_ge(h, v))
    def wT(sem, v): w(qT, sem, v)
    def wV(sem, v): w(qV, sem, v)
    def wS(sem, v): w(qS, sem, v)
    def wG(sem, v): w(qG, sem, v)
    def wY(sem, v): w(qY, sem, v)

    def vflush(): wV(sV, sV.n)
    def sflush(): wS(sS, sS.n)
    def gflush(): wG(sG, sG.n)

    def barrier():
        snap = [(s.h, s.n) for s in ALLSEMS if s.n > 0]
        for q in (qT, qV, qS, qG, qY):
            for h, n in snap:
                q.append(lambda e, h=h, n=n: e.wait_ge(h, n))

    A = nc.alloc_sbuf_tensor

    xnT = A("xnT", [128, DC * S], BF16)
    slab = A("slab", [128, 8 * S], BF16)
    identF = A("identF", [128, 128], F32)
    identB = A("identB", [128, 128], BF16)

    biasfull = A("biasfull", [128, 32], F32)
    b1T = A("b1T", [128, 128], F32)

    brS = A("brS", [8, 1], F32)
    wrS = A("wrS", [128, DC * E], F32)
    gT = A("gT", [8, S], BF16)
    epsb = A("epsb", [128, 1], F32)

    p0 = ExitStack()
    scr0 = p0.enter_context(nc.sbuf_tensor("scr0", [128, 1024], F32))
    brow = p0.enter_context(nc.sbuf_tensor("brow", [32, 128], F32))
    b1row = p0.enter_context(nc.sbuf_tensor("b1row", [128, 128], F32))
    psA0 = p0.enter_context(nc.psum_tensor("psA0", [128, 128], F32))
    psB0 = p0.enter_context(nc.psum_tensor("psB0", [128, 128], F32))

    eY(lambda y: y.dma_start(out=brow[0:8, :], in_=bq_h.rearrange("(pr two) h -> pr (two h)", two=2)), sIni)
    eY(lambda y: y.dma_start(out=brow[8:16, :], in_=bk_h.rearrange("(pr two) h -> pr (two h)", two=2)), sIni)
    eY(lambda y: y.dma_start(out=brow[16:24, :], in_=bv_h.rearrange("(pr two) h -> pr (two h)", two=2)), sIni)
    eY(lambda y: y.dma_start(out=brow[24:32, :], in_=bo_h.rearrange("(ec p) -> ec p", p=128)), sIni)
    eY(lambda y: y.dma_start(out=b1row[:, :], in_=b1_h.rearrange("e (fc p) -> (e fc) p", p=128)), sIni)
    for dc in range(DC):
        eY(lambda y, dc=dc: y.dma_start(out=wrS[:, dc * 8:(dc + 1) * 8],
                                        in_=wr_h[dc * 128:(dc + 1) * 128, :]), sIni)
    eY(lambda y: y.dma_start(out=brS[0:8, 0:1], in_=br_h.rearrange("(e one) -> e one", one=1)), sIni)
    v_init_loads = sIni.n


    eG(lambda g: g.iota(scr0[:, 0:128], [[1, 128]], base=0, channel_multiplier=-1,
                        allow_small_or_imprecise_dtypes=True), sG, 1)
    eG(lambda g: g.memset(epsb[:, :], 1e-5), sG, 1)
    v_g_ident = sG.n
    v_g_all = sG.n

    wV(sG, v_g_ident)
    eV(lambda v: v.tensor_single_scalar(identF[:, :], scr0[:, 0:128], 0.0, ALU.is_equal), sV)
    vflush()
    eV(lambda v: v.tensor_copy(identB[:, :], identF[:, :]), sV)
    v_ident = sV.n
    wV(sG, v_g_all)

    wT(sIni, v_init_loads)
    wT(sV, v_ident)
    eT(lambda t: t.transpose(psA0[:, 0:32], brow[:, :], identF[0:32, 0:32]), sT)
    eT(lambda t: t.transpose(psB0[:, :], b1row[:, :], identF[:, :]), sT)
    v_btr = sT.n
    wV(sT, v_btr)
    eV(lambda v: v.tensor_copy(biasfull[:, 0:32], psA0[:, 0:32]), sV)
    eV(lambda v: v.tensor_copy(b1T[:, :], psB0[:, :]), sV)

    barrier()
    p0.close()
    if upto < 1:
        return _finish(nc, st, taps, (qT, qV, qS, qG, qY), sO, locals())

    p1 = ExitStack()
    Xb = [p1.enter_context(nc.sbuf_tensor(f"Xb{i}", [128, D], F32)) for i in range(2)]
    xnb = [p1.enter_context(nc.sbuf_tensor(f"xnb{i}", [128, D], F32)) for i in range(2)]
    ttr_scr = p1.enter_context(nc.sbuf_tensor("ttr_scr", [128, D], F32))
    win = [p1.enter_context(nc.sbuf_tensor(f"win{i}", [128, DC * 512], F32)) for i in range(2)]
    s1 = p1.enter_context(nc.sbuf_tensor("s1", [128, 1], F32))
    m2 = p1.enter_context(nc.sbuf_tensor("m2", [128, 1], F32))
    meanb = p1.enter_context(nc.sbuf_tensor("meanb", [128, 1], F32))
    msq = p1.enter_context(nc.sbuf_tensor("msq", [128, 1], F32))
    varb = p1.enter_context(nc.sbuf_tensor("varb", [128, 1], F32))
    lnv = p1.enter_context(nc.sbuf_tensor("lnv", [128, 1], F32))
    rstd = p1.enter_context(nc.sbuf_tensor("rstd", [128, 1], F32))
    lgT = p1.enter_context(nc.sbuf_tensor("lgT", [8, 512], F32))
    lgtok = p1.enter_context(nc.sbuf_tensor("lgtok", [128, 8], F32))
    mx = p1.enter_context(nc.sbuf_tensor("mx", [128, 1], F32))
    nmx = p1.enter_context(nc.sbuf_tensor("nmx", [128, 1], F32))
    exs = p1.enter_context(nc.sbuf_tensor("exs", [128, 8], F32))
    sume = p1.enter_context(nc.sbuf_tensor("sume", [128, 1], F32))
    epb = p1.enter_context(nc.sbuf_tensor("epb", [128, 1], F32))
    msks = p1.enter_context(nc.sbuf_tensor("msks", [128, 8], F32))
    gtok = p1.enter_context(nc.sbuf_tensor("gtok", [128, 8], F32))
    psTr = [p1.enter_context(nc.psum_tensor(f"psTr{i}", [128, 128], F32)) for i in range(2)]
    psR = p1.enter_context(nc.psum_tensor("psR", [8, 512], F32))
    psLg = p1.enter_context(nc.psum_tensor("psLg", [128, 8], F32))
    psG1 = p1.enter_context(nc.psum_tensor("psG1", [8, 128], F32))

    xb_free = [0, 0]
    xb_free_S = [0, 0]
    xn_last_T = [0, 0]
    tr_evac_V = [0, 0]
    win_free_T = [0, 0]
    win_free_S = [0, 0]
    psR_evac = 0
    psLg_free = 0
    psG_free = 0
    tr_idx = 0

    for i in range(ntt):
        sl = i % 2
        wsl = (i // 4) % 2
        if sl == 0:
            wY(sV, xb_free[sl])
            wY(sS, xb_free_S[sl])
            eY(lambda y, i=i, sl=sl: y.dma_start(out=Xb[sl][:, :],
                                                 in_=x_h[i * 128:(i + 1) * 128, :]), sX[sl])
        else:
            wG(sV, xb_free[sl])
            wG(sS, xb_free_S[sl])
            eG(lambda g, i=i, sl=sl: g.dma_start(out=Xb[sl][:, :],
                                                 in_=x_h[i * 128:(i + 1) * 128, :]), sX[sl])
        v_x = sX[sl].n
        wV(sX[sl], v_x)
        eV(lambda v, sl=sl: v.reduce_sum(s1[:, 0:1], Xb[sl][:, :], axis=AX.X), sV)
        wS(sX[sl], v_x)
        eS(lambda s, sl=sl: s.activation(ttr_scr[:, :], Xb[sl][:, :], AF.Square,
                                         accum_out=m2[:, 0:1]), sS)
        v_sq = sS.n
        xb_free_S[sl] = v_sq
        vflush()
        eV(lambda v: v.tensor_single_scalar(meanb[:, 0:1], s1[:, 0:1], 1.0 / D, ALU.mult), sV)
        vflush()
        eV(lambda v: v.tensor_tensor(msq[:, 0:1], meanb[:, 0:1], meanb[:, 0:1], ALU.mult), sV)
        wV(sS, v_sq)
        eV(lambda v: v.tensor_single_scalar(m2[:, 0:1], m2[:, 0:1], 1.0 / D, ALU.mult), sV)
        vflush()
        eV(lambda v: v.tensor_tensor(varb[:, 0:1], m2[:, 0:1], msq[:, 0:1], ALU.subtract), sV)
        v_var = sV.n
        wS(sV, v_var)
        eS(lambda s: s.activation(lnv[:, 0:1], varb[:, 0:1], AF.Ln, bias=epsb[:, 0:1]), sS)
        sflush()
        eS(lambda s: s.activation(rstd[:, 0:1], lnv[:, 0:1], AF.Exp, scale=-0.5), sS)
        v_rstd = sS.n
        wV(sS, v_rstd)
        wV(sT, xn_last_T[sl])
        eV(lambda v, sl=sl: v.tensor_scalar(
            xnb[sl][:, :], Xb[sl][:, :], meanb[:, 0:1], rstd[:, 0:1],
            ALU.subtract, ALU.mult), sV)
        v_xn = sV.n
        xb_free[sl] = v_xn
        for dc in range(DC):
            bk = tr_idx % 2
            tr_idx += 1
            wT(sV, v_xn)
            wT(sV, tr_evac_V[bk])
            eT(lambda t, sl=sl, dc=dc, bk=bk: t.transpose(
                psTr[bk][:, :], xnb[sl][:, dc * 128:(dc + 1) * 128], identF[:, :]), sT)
            v_tr = sT.n
            wV(sT, v_tr)
            if i % 4 == 0 and dc == 0:
                wV(sT, win_free_T[wsl])
                wV(sS, win_free_S[wsl])
            eV(lambda v, dc=dc, i=i, bk=bk, wsl=wsl: v.tensor_copy(
                win[wsl][:, dc * 512 + (i % 4) * 128: dc * 512 + (i % 4) * 128 + 128],
                psTr[bk][:, :]), sV)
            tr_evac_V[bk] = sV.n
            wS(sV, tr_evac_V[bk])
            eS(lambda s, dc=dc, i=i, wsl=wsl: s.copy(
                xnT[:, dc * S + i * 128: dc * S + (i + 1) * 128],
                win[wsl][:, dc * 512 + (i % 4) * 128: dc * 512 + (i % 4) * 128 + 128]), sS)
        xn_last_T[sl] = sT.n
        if i % 4 == 3:
            win_free_S[wsl] = sS.n

        if i % 4 == 3 and rtr:
            v_win = sV.n
            wT(sV, v_win)
            wT(sS, psR_evac)
            for dc in range(DC):
                eT(lambda t, dc=dc, wsl=wsl: t.matmul(
                    psR[0:8, :], wrS[:, dc * 8:(dc + 1) * 8],
                    win[wsl][:, dc * 512:(dc + 1) * 512],
                    start=(dc == 0), stop=(dc == DC - 1)),
                   sem=(sT if dc == DC - 1 else None))
            win_free_T[wsl] = sT.n
            v_rmm = sT.n
            wS(sT, v_rmm)
            eS(lambda s: s.activation(lgT[0:8, :], psR[0:8, :], AF.Identity,
                                      bias=brS[0:8, 0:1]), sS)
            psR_evac = sS.n
            v_lg = sS.n
            for j in range(4):
                tt = (i // 4) * 4 + j
                wT(sS, v_lg)
                wT(sV, psLg_free)
                eT(lambda t, j=j: t.transpose(psLg[:, 0:8], lgT[0:8, j * 128:(j + 1) * 128],
                                              identF[0:8, 0:8]), sT)
                v_lgt = sT.n
                wV(sT, v_lgt)
                eV(lambda v: v.tensor_copy(lgtok[:, 0:8], psLg[:, 0:8]), sV)
                psLg_free = sV.n
                vflush()
                eV(lambda v: v.reduce_max(mx[:, 0:1], lgtok[:, 0:8], axis=AX.X), sV)
                vflush()
                eV(lambda v: v.tensor_single_scalar(nmx[:, 0:1], mx[:, 0:1], -1.0, ALU.mult), sV)
                v_mx = sV.n
                wS(sV, v_mx)
                eS(lambda s: s.activation(exs[:, 0:8], lgtok[:, 0:8], AF.Exp,
                                          bias=nmx[:, 0:1], accum_out=sume[:, 0:1]), sS)
                v_se = sS.n
                wV(sS, v_se)
                eV(lambda v: v.reciprocal(epb[:, 0:1], sume[:, 0:1]), sV)
                eV(lambda v: v.tensor_single_scalar(msks[:, 0:8], lgtok[:, 0:8],
                                                    mx[:, 0:1], ALU.is_ge), sV)
                vflush()
                eV(lambda v: v.tensor_scalar(gtok[:, 0:8], msks[:, 0:8], epb[:, 0:1], None,
                                             ALU.mult), sV)
                v_g = sV.n
                wT(sV, v_g)
                wT(sV, psG_free)
                eT(lambda t: t.transpose(psG1[0:8, :], gtok[:, 0:8], identF[:, :]), sT)
                v_gt = sT.n
                wV(sT, v_gt)
                eV(lambda v, tt=tt: v.tensor_copy(gT[0:8, tt * 128:(tt + 1) * 128],
                                                  psG1[0:8, :]), sV)
                psG_free = sV.n

    barrier()
    p1.close()
    if upto < 2:
        return _finish(nc, st, taps, (qT, qV, qS, qG, qY), sO, locals())

    p2 = ExitStack()
    wqkv = [p2.enter_context(nc.sbuf_tensor("wqkv0", [128, 3 * 1024], BF16))]
    KTs = p2.enter_context(nc.sbuf_tensor("KTs", [128, S], BF16))
    QTs = p2.enter_context(nc.sbuf_tensor("QTs", [128, S], BF16))
    VTs = [p2.enter_context(nc.sbuf_tensor(f"VTs{i}", [128, 512], BF16)) for i in range(2)]
    Vp0 = p2.enter_context(nc.sbuf_tensor("Vp0", [128, NKC * 128], BF16))
    Vp1 = p2.enter_context(nc.sbuf_tensor("Vp1", [128, NKC * 128], BF16))
    Pt = [p2.enter_context(nc.sbuf_tensor(f"Pt{i}", [128, 2048], BF16)) for i in range(2)]
    sufSB = p2.enter_context(nc.sbuf_tensor("sufSB", [1, NQB * 256], BF16))
    ones512 = p2.enter_context(nc.sbuf_tensor("ones512", [128, 512], BF16))
    onescolB = p2.enter_context(nc.sbuf_tensor("onescolB", [128, 1], BF16))
    Zbuf = [p2.enter_context(nc.sbuf_tensor(f"Zbuf{i}", [96, 512], F32)) for i in range(2)]
    rzbsb = p2.enter_context(nc.sbuf_tensor("rzbsb", [128, 512], F32))
    sufacc = p2.enter_context(nc.sbuf_tensor("sufacc", [1, 256], F32))
    masks = p2.enter_context(nc.sbuf_tensor("masks", [128, 4 * 1024], BF16))
    spairB = p2.enter_context(nc.psum_tensor("spairB", [128, 2048], F32))

    for j in range(2):
        eG(lambda g, j=j: g.iota(
            masks.ap().rearrange("p (j d two f) -> p j d two f", j=2, d=2, two=2)[:, j],
            [[-128, 2], [0, 2], [1, 512]], base=-(2 * j * 128), channel_multiplier=-1,
            allow_small_or_imprecise_dtypes=True), sG, 1)
    v_g_masks = sG.n
    wV(sG, v_g_masks)
    for j in range(2):
        eV(lambda v, j=j: v.tensor_single_scalar(
            masks[:, j * 2048:(j + 1) * 2048], masks[:, j * 2048:(j + 1) * 2048],
            0.0, ALU.is_ge), sV)
    vflush()
    v_masks_done = sV.n

    eG(lambda g: g.memset(onescolB[:, :], 1.0), sG, 1)
    eG(lambda g: g.memset(ones512[:, :], 1.0), sG, 1)
    eG(lambda g: g.memset(Vp0[:, :], 0.0), sG, 1)
    gflush()
    eG(lambda g: g.memset(
        Vp0.ap().rearrange("p (kc c) -> p kc c", c=128)[:, :, 64:65], 1.0), sG, 1)
    eG(lambda g: g.memset(Vp1[:, :], 0.0), sG, 1)
    gflush()
    eG(lambda g: g.memset(
        Vp1.ap().rearrange("p (kc c) -> p kc c", c=128)[:, :, 0:1], 1.0), sG, 1)
    v_vpinit = sG.n
    wT(sG, v_vpinit)
    wV(sG, v_vpinit)

    def qkv_dma(p):
        for m, wh in ((0, wk_h), (1, wq_h), (2, wv_h)):
            for hh in range(2):
                n = 2 * p + hh
                eG(lambda g, m=m, wh=wh, n=n, hh=hh: g.dma_start(
                    out=wqkv[0].ap().rearrange("q (m dc h) -> q m dc h", m=3, dc=DC)
                        [:, m, :, hh * 64:(hh + 1) * 64],
                    in_=wh[n].rearrange("(dc q) h -> q dc h", q=128)), sWq[p % 2])
        return sWq[p % 2].n

    wq_ready = [0, 0]
    wq_free_T = [0, 0]
    ktqt_free_T = 0
    vp_free_T = 0
    spair_slot_S = [0, 0]
    spair_exp_S = 0
    pv_done = [0, 0]
    o_free_V = 0
    o_free_S = 0
    norm_done_V = 0
    zbuf_free_V = [0, 0]
    zsl = 0
    wq_ready[0] = qkv_dma(0)

    for p in range(NP):
        pA = ExitStack()
        psVt = [pA.enter_context(nc.psum_tensor(f"psVt{p}_{i}", [128, 128], BF16)) for i in range(2)]
        wT(sWq[p % 2], wq_ready[p % 2])
        wS(sT, ktqt_free_T)
        wV(sT, vp_free_T)
        wT(sV, max(o_free_V, norm_done_V))
        wS(sV, max(o_free_V, norm_done_V))
        bidx = 0
        vts_free_T = [0, 0]
        vtr_free_V = [0, 0]
        for m, dst, bcol in ((0, KTs, 8), (1, QTs, 0), (2, None, 16)):
            for kb in range(NQB):
                bk = bidx % 2
                bidx += 1
                wT(sS, spair_slot_S[bk])
                for dc in range(DC):
                    eT(lambda t, p=p, m=m, dc=dc, kb=kb, bk=bk: t.matmul(
                        spairB[:, bk * 1024: bk * 1024 + 512],
                        wqkv[0][:, m * 1024 + dc * 128: m * 1024 + (dc + 1) * 128],
                        xnT[:, dc * S + kb * 512: dc * S + (kb + 1) * 512],
                        start=(dc == 0), stop=(dc == DC - 1)),
                       sem=(sT if dc == DC - 1 else None))
                v_mm = sT.n
                wS(sT, v_mm)
                if m < 2:
                    eS(lambda s, dst=dst, kb=kb, bk=bk, bcol=bcol, p=p: s.activation(
                        dst[:, kb * 512:(kb + 1) * 512],
                        spairB[:, bk * 1024: bk * 1024 + 512], AF.Identity,
                        bias=biasfull[:, bcol + p: bcol + p + 1]), sS)
                    spair_slot_S[bk] = sS.n
                else:
                    vsl = kb % 2
                    wS(sT, vts_free_T[vsl])
                    eS(lambda s, vsl=vsl, bk=bk, bcol=bcol, p=p: s.activation(
                        VTs[vsl][:, :], spairB[:, bk * 1024: bk * 1024 + 512], AF.Identity,
                        bias=biasfull[:, bcol + p: bcol + p + 1]), sS)
                    spair_slot_S[bk] = sS.n
                    v_vts = sS.n
                    for j in range(4):
                        kc = kb * 4 + j
                        tb = kc % 2
                        wT(sS, v_vts)
                        wT(sV, vtr_free_V[tb])
                        eT(lambda t, j=j, tb=tb, vsl=vsl: t.transpose(
                            psVt[tb][:, :], VTs[vsl][:, j * 128:(j + 1) * 128],
                            identB[:, :]), sT)
                        v_tr = sT.n
                        wV(sT, v_tr)
                        eV(lambda v, kc=kc, tb=tb: v.tensor_copy(
                            Vp0[:, kc * 128: kc * 128 + 64], psVt[tb][:, 0:64]), sV)
                        eV(lambda v, kc=kc, tb=tb: v.tensor_copy(
                            Vp1[:, kc * 128 + 64: kc * 128 + 128], psVt[tb][:, 64:128]), sV)
                        vtr_free_V[tb] = sV.n
                    vts_free_T[vsl] = sT.n
        wq_free_T[p % 2] = sT.n
        v_vp_done = sV.n
        pA.close()
        if p + 1 < NP:
            wG(sT, wq_free_T[p % 2])
            wq_ready[(p + 1) % 2] = qkv_dma(p + 1)

        pB = ExitStack()
        psO0 = pB.enter_context(nc.psum_tensor(f"psO0_{p}", [128, 512], F32))
        psO1 = pB.enter_context(nc.psum_tensor(f"psO1_{p}", [128, 512], F32))
        psSuf = pB.enter_context(nc.psum_tensor(f"psSuf_{p}", [1, 256], F32))
        pending = []

        def emit_norm():
            nonlocal norm_done_V
            ep, eqb, ezsl, zready = pending.pop(0)
            wV(sS, zready[1])
            for dst0, src0 in ((0, 64), (32, 64), (64, 0), (96, 0)):
                eV(lambda v, ezsl=ezsl, dst0=dst0, src0=src0: v.stream_shuffle(
                    rzbsb[dst0:dst0 + 32, :], Zbuf[ezsl][src0:src0 + 32, :],
                    [0] * 32), sV)
            zbuf_free_V[ezsl] = sV.n
            vflush()
            eV(lambda v: v.reciprocal(rzbsb[:, :], rzbsb[:, :]), sV)
            vflush()
            eV(lambda v, ep=ep, eqb=eqb: v.tensor_tensor(
                slab[:, ep * S + eqb * 512: ep * S + (eqb + 1) * 512],
                slab[:, ep * S + eqb * 512: ep * S + (eqb + 1) * 512],
                rzbsb[:, :], ALU.mult), sV)
            norm_done_V = sV.n

        wT(sV, v_vp_done)
        suf_evac_V = 0
        suf_ready = {}
        suf_steps = list(range(NQB - 2, -1, -1))

        def emit_suf_step():
            nonlocal suf_evac_V
            sqb = suf_steps.pop(0)
            wT(sV, suf_evac_V)
            cs = list(range(4 * sqb + 4, 4 * sqb + 8))
            for ci, c in enumerate(cs):
                eT(lambda t, c=c, s0=(ci == 0): t.matmul(
                    psSuf[0:1, 0:128], onescolB[:, 0:1], Vp0[:, c * 128:(c + 1) * 128],
                    start=s0, stop=False))
                eT(lambda t, c=c, e0=(ci == len(cs) - 1): t.matmul(
                    psSuf[0:1, 128:256], onescolB[:, 0:1], Vp1[:, c * 128:(c + 1) * 128],
                    start=False, stop=e0), sT)
            v_sufmm = sT.n
            wV(sT, v_sufmm)
            if sqb == NQB - 2:
                eV(lambda v: v.tensor_copy(sufacc[0:1, :], psSuf[0:1, 0:256]), sV)
            else:
                eV(lambda v: v.tensor_tensor(
                    sufacc[0:1, :], psSuf[0:1, 0:256], sufacc[0:1, :], ALU.add), sV)
            vflush()
            eV(lambda v, sqb=sqb: v.tensor_copy(
                sufSB[0:1, sqb * 256:(sqb + 1) * 256], sufacc[0:1, :]), sV)
            suf_evac_V = sV.n
            suf_ready[sqb] = sV.n

        rc = 0
        for qb in range(NQB - 1, -1, -1):
            reals = list(range(0, 4 * qb + 4))
            ofirst = True

            def emit_smm(idx):
                nonlocal spair_exp_S
                c = reals[idx]
                sl = (rc0 + idx) % 2
                wT(sS, spair_slot_S[sl])
                eT(lambda t, c=c, sl=sl, qb=qb: t.matmul(
                    spairB[:, sl * 1024: sl * 1024 + 512],
                    KTs[0:64, c * 128:(c + 1) * 128],
                    QTs[0:64, qb * 512:(qb + 1) * 512], start=True, stop=True))
                eT(lambda t, c=c, sl=sl, qb=qb: t.matmul(
                    spairB[:, sl * 1024 + 512: sl * 1024 + 1024],
                    KTs[64:128, c * 128:(c + 1) * 128],
                    QTs[64:128, qb * 512:(qb + 1) * 512], start=True, stop=True), sT)
                v_smm = sT.n
                if c >= 4 * qb:
                    d = c - 4 * qb
                    moff = (d // 2) * 2048 + (d % 2) * 1024
                    wV(sT, v_smm)
                    eV(lambda v, sl=sl, moff=moff: v.tensor_tensor(
                        spairB[:, sl * 1024:(sl + 1) * 1024],
                        spairB[:, sl * 1024:(sl + 1) * 1024],
                        masks[:, moff: moff + 1024], ALU.mult), sV)
                    wS(sV, sV.n)
                else:
                    wS(sT, v_smm)
                wS(sT, pv_done[sl])
                eS(lambda s, sl=sl: s.activation(
                    Pt[sl][:, 0:1024], spairB[:, sl * 1024:(sl + 1) * 1024],
                    AF.Exp, scale=0.125), sS)
                spair_slot_S[sl] = sS.n
                spair_exp_S = sS.n

            def emit_pv(idx):
                c = reals[idx]
                sl = (rc0 + idx) % 2
                wT(sS, spair_slot_S[sl])
                first = ofirst and idx == 0
                last = idx == len(reals) - 1
                eT(lambda t, c=c, sl=sl, first=first, last=last: t.matmul(
                    psO0[:, :], Vp0[:, c * 128:(c + 1) * 128],
                    Pt[sl][:, 0:512], start=first, stop=last))
                eT(lambda t, c=c, sl=sl, first=first, last=last: t.matmul(
                    psO1[:, :], Vp1[:, c * 128:(c + 1) * 128],
                    Pt[sl][:, 512:1024], start=first, stop=last), sT)
                pv_done[sl] = sT.n

            rc0 = rc
            emit_smm(0)
            if len(reals) > 1:
                emit_smm(1)
            wT(sV, o_free_V)
            wT(sS, o_free_S)
            if qb < NQB - 1:
                wT(sV, suf_ready[qb])
                eT(lambda t, qb=qb: t.matmul(
                    psO0[:, :], sufSB[0:1, qb * 256: qb * 256 + 128], ones512[0:1, :],
                    start=True, stop=False))
                eT(lambda t, qb=qb: t.matmul(
                    psO1[:, :], sufSB[0:1, qb * 256 + 128: qb * 256 + 256], ones512[0:1, :],
                    start=True, stop=False), sT)
                ofirst = False
            for idx in range(len(reals)):
                emit_pv(idx)
                if qb == NQB - 1 and suf_steps and idx % 4 == 1:
                    emit_suf_step()
                if idx + 2 < len(reals):
                    emit_smm(idx + 2)
            rc += len(reals)
            pv_last = sT.n
            wV(sT, pv_last)
            wS(sT, pv_last)
            wV(sV, zbuf_free_V[zsl])
            wS(sV, zbuf_free_V[zsl])
            eV(lambda v, p=p, qb=qb: v.tensor_copy(
                slab[0:64, p * S + qb * 512: p * S + (qb + 1) * 512], psO0[0:64, :]), sV)
            eS(lambda s, zsl=zsl: s.copy(Zbuf[zsl][0:1, :], psO1[0:1, :]), sS)
            v_ph1_v = sV.n
            v_ph1_s = sS.n
            wV(sS, v_ph1_s)
            wS(sV, v_ph1_v)
            eV(lambda v, p=p, qb=qb: v.tensor_copy(
                slab[64:128, p * S + qb * 512: p * S + (qb + 1) * 512],
                psO1[64:128, :]), sV)
            eS(lambda s, zsl=zsl: s.copy(Zbuf[zsl][64:65, :], psO0[64:65, :]), sS)
            o_free_V = sV.n
            o_free_S = sS.n
            zb_ready = (sV.n, sS.n)
            pending.append((p, qb, zsl, zb_ready))
            zsl ^= 1
            if len(pending) > 1:
                emit_norm()
        while pending:
            emit_norm()
        pB.close()
        ktqt_free_T = sT.n
        vp_free_T = sT.n

    barrier()
    p2.close()
    if upto < 3:
        return _finish(nc, st, taps, (qT, qV, qS, qG, qY), sO, locals())

    p3 = ExitStack()
    woT = p3.enter_context(nc.sbuf_tensor("woT", [128, NP * 1024], BF16))
    psP3 = [p3.enter_context(nc.psum_tensor(f"psP3{i}", [128, 512], F32)) for i in range(8)]
    for p in range(NP):
        eG(lambda g, p=p: g.dma_start(
            out=woT[:, p * 1024:(p + 1) * 1024],
            in_=wo_h[2 * p:2 * p + 2].rearrange("two h e -> (two h) e")), sWo)
    wT(sWo, sWo.n)
    p3_evac = [0] * 8
    for qb in range(NQB):
        for ec in range(8):
            wT(sV, p3_evac[ec])
            for p in range(NP):
                eT(lambda t, ec=ec, p=p, qb=qb: t.matmul(
                    psP3[ec][:, :], woT[:, p * 1024 + ec * 128: p * 1024 + (ec + 1) * 128],
                    slab[:, p * S + qb * 512: p * S + (qb + 1) * 512],
                    start=(p == 0), stop=(p == NP - 1)),
                   sem=(sT if p == NP - 1 else None))
        v_all = sT.n
        for ec in range(8):
            wV(sT, v_all)
            eV(lambda v, ec=ec, qb=qb: v.tensor_scalar(
                slab[:, ec * S + qb * 512: ec * S + (qb + 1) * 512],
                psP3[ec][:, :], biasfull[:, 24 + ec: 25 + ec], None, ALU.add), sV)
            p3_evac[ec] = sV.n

    barrier()
    p3.close()
    if upto < 4:
        return _finish(nc, st, taps, (qT, qV, qS, qG, qY), sO, locals())

    p4 = ExitStack()
    w1sb = p4.enter_context(nc.sbuf_tensor("w1sb", [128, DC * F], BF16))
    oneh = p4.enter_context(nc.sbuf_tensor("oneh", [8, E * 128], BF16))
    b2bf = p4.enter_context(nc.sbuf_tensor("b2bf", [8, D], BF16))
    w2sb = [p4.enter_context(nc.sbuf_tensor(f"w2sb{i}", [128, FC * 128], BF16)) for i in range(3)]
    htil = p4.enter_context(nc.sbuf_tensor("htil", [128, FC * 512], BF16))
    gb = [p4.enter_context(nc.sbuf_tensor(f"gb{i}", [128, 512], BF16)) for i in range(2)]
    psH = [p4.enter_context(nc.psum_tensor(f"psH{i}", [128, 512], F32)) for i in range(2)]
    psY = [p4.enter_context(nc.psum_tensor(f"psY{i}", [128, 512], F32)) for i in range(2)]
    psB4 = p4.enter_context(nc.psum_tensor("psB4", [128, 512], F32))
    Xb5 = [p4.enter_context(nc.sbuf_tensor(f"Xb5{i}", [128, 512], F32)) for i in range(2)]
    psF = [p4.enter_context(nc.psum_tensor(f"psF{i}", [128, 512], BF16)) for i in range(2)]

    w1_ready = [0, 0]
    w1_free_T = [0, 0]
    w2_free_T = [0, 0, 0]
    h_evac_S = [0, 0]
    h_gate_V = [0] * FC
    h_used_T = [0] * FC
    y_evac_V = [0, 0]
    gb_free_V = [0, 0]
    psB_evac_S = 0
    w2_wb = [0, 0, 0]

    def w1_dma(e, half):
        f0 = half * 1024
        wG(sT, w1_free_T[half])
        eG(lambda g, e=e, f0=f0: g.dma_start(
            out=w1sb.ap().rearrange("q (dc f) -> q dc f", dc=DC)[:, :, f0:f0 + 1024],
            in_=w1_h[e].rearrange("(dc q) f -> q dc f", q=128)[:, :, f0:f0 + 1024]),
            sW1[half])
        w1_ready[half] = sW1[half].n

    eG(lambda g: g.dma_start(out=b2bf[0:8, :], in_=b2_h[:, :]), sB2)
    eG(lambda g: g.iota(oneh.ap().rearrange("p (e c) -> p e c", e=E),
                        [[0, E], [0, 128]], base=0, channel_multiplier=1,
                        allow_small_or_imprecise_dtypes=True), sG, 1)
    v_onehramp = sG.n
    wV(sG, v_onehramp)
    for e in range(E):
        eV(lambda v, e=e: v.tensor_single_scalar(
            oneh[0:8, e * 128:(e + 1) * 128], oneh[0:8, e * 128:(e + 1) * 128],
            float(e), ALU.is_equal), sV)
    vflush()
    v_oneh_done = sV.n
    wT(sV, v_oneh_done)
    wT(sB2, sB2.n)
    w1_dma(0, 0)
    w1_dma(0, 1)

    x5_free = [0, 0]
    x5_out = [0, 0]
    x5_ready = {}
    psF_evac = [0, 0]
    p5_gidx = 0
    y_done_qb = [0] * NQB

    p5_groups = [(4 * qq + k // 2, k % 2) for qq in range(NQB) for k in range(8)]
    p5_load_ptr = 0
    p5_cons_ptr = 0

    def emit_p5_load_next():
        nonlocal p5_load_ptr
        if p5_load_ptr >= len(p5_groups):
            return
        i, g2 = p5_groups[p5_load_ptr]
        sl = p5_load_ptr % 2
        p5_load_ptr += 1
        wG(sV, x5_free[sl])
        wG(sO[sl], x5_out[sl])
        eG(lambda g, i=i, g2=g2, sl=sl: g.dma_start(
            out=Xb5[sl][:, :],
            in_=x_h[i * 128:(i + 1) * 128, g2 * 512:(g2 + 1) * 512]), sX[sl])
        x5_ready[(i, g2)] = sX[sl].n

    def emit_p5_group(yv):
        nonlocal p5_cons_ptr, p5_gidx
        i, g2 = p5_groups[p5_cons_ptr]
        sl = p5_cons_ptr % 2
        p5_cons_ptr += 1
        bk = p5_gidx % 2
        p5_gidx += 1
        wV(sX[sl], x5_ready[(i, g2)])
        wT(sV, psF_evac[bk])
        wT(sV, yv)
        for j in range(4):
            dc = g2 * 4 + j
            eT(lambda t, i=i, dc=dc, bk=bk, j=j: t.transpose(
                psF[bk][:, j * 128:(j + 1) * 128],
                slab[:, dc * S + i * 128: dc * S + (i + 1) * 128],
                identB[:, :]), sem=(sT if j == 3 else None))
        v_tr = sT.n
        wV(sT, v_tr)
        eV(lambda v, sl=sl, bk=bk: v.tensor_tensor(
            Xb5[sl][:, :], psF[bk][:, :], Xb5[sl][:, :], ALU.add), sV)
        psF_evac[bk] = sV.n
        x5_free[sl] = sV.n
        wG(sV, sV.n)
        eG(lambda g, i=i, g2=g2, sl=sl: g.dma_start(
            out=out_h[i * 128:(i + 1) * 128, g2 * 512:(g2 + 1) * 512],
            in_=Xb5[sl][:, :]), sO[sl])
        x5_out[sl] = sO[sl].n
        emit_p5_load_next()

    for e in range(E):
        for qb in range(NQB):
            if e == E - 1 and qb == 0:
                emit_p5_load_next()
                emit_p5_load_next()
            gsl = qb % 2
            wT(sV, gb_free_V[gsl])
            wT(sS, psB_evac_S)
            eT(lambda t, e=e, qb=qb: t.matmul(
                psB4[:, :], oneh[0:8, e * 128:(e + 1) * 128],
                gT[0:8, qb * 512:(qb + 1) * 512], start=True, stop=True), sT)
            v_gbm = sT.n
            wS(sT, v_gbm)
            eS(lambda s, gsl=gsl: s.copy(gb[gsl][:, :], psB4[:, :]), sS)
            psB_evac_S = sS.n
            gb_ready = sS.n
            for fc in range(FC):
                bk = fc % 2
                half = fc // 8
                if fc % 8 == 0:
                    wT(sW1[half], w1_ready[half])
                wT(sS, h_evac_S[bk])
                for dc in range(DC):
                    eT(lambda t, fc=fc, dc=dc, qb=qb, bk=bk: t.matmul(
                        psH[bk][:, :],
                        w1sb[:, dc * F + fc * 128: dc * F + (fc + 1) * 128],
                        xnT[:, dc * S + qb * 512: dc * S + (qb + 1) * 512],
                        start=(dc == 0), stop=(dc == DC - 1)),
                       sem=(sT if dc == DC - 1 else None))
                if qb == NQB - 1 and fc % 8 == 7:
                    w1_free_T[half] = sT.n
                v_hmm = sT.n
                wS(sT, v_hmm)
                wS(sT, h_used_T[fc])
                eS(lambda s, e=e, fc=fc, bk=bk: s.activation(
                    htil[:, fc * 512:(fc + 1) * 512], psH[bk][:, :], AF.Relu,
                    bias=b1T[:, e * 16 + fc: e * 16 + fc + 1]), sS)
                h_evac_S[bk] = sS.n
                v_relu = sS.n
                wV(sS, v_relu)
                wV(sS, gb_ready)
                eV(lambda v, fc=fc, gsl=gsl: v.tensor_tensor(
                    htil[:, fc * 512:(fc + 1) * 512], htil[:, fc * 512:(fc + 1) * 512],
                    gb[gsl][:, :], ALU.mult), sV)
                h_gate_V[fc] = sV.n
                if qb == NQB - 1 and e + 1 < E and fc % 8 == 7:
                    w1_dma(e + 1, fc // 8)
            gb_free_V[gsl] = sV.n
            for dcb in range(DC):
                sl2 = dcb % 3
                if qb == 0:
                    wG(sT, w2_free_T[sl2])
                    wG(sW2w[sl2], w2_wb[sl2])
                    eG(lambda g, e=e, dcb=dcb, sl2=sl2: g.dma_start(
                        out=w2sb[sl2].ap().rearrange("q (fc d) -> q fc d", fc=FC),
                        in_=w2_h[e].rearrange("(fc q) d -> q fc d", q=128)
                            [:, :, dcb * 128:(dcb + 1) * 128]), sW2[sl2])
                    v_w2 = sW2[sl2].n
                    wY(sW2[sl2], v_w2)

                    eY(lambda y, e=e, dcb=dcb, sl2=sl2: y.dma_start(
                        out=w2c_h[e][dcb], in_=w2sb[sl2][:, :]), sW2w[sl2])
                    w2_wb[sl2] = sW2w[sl2].n
                    v_w2s = (sW2[sl2], v_w2)
                else:
                    wY(sT, w2_free_T[sl2])
                    wY(sW2w[sl2], w2_wb[sl2])
                    eY(lambda y, e=e, dcb=dcb, sl2=sl2: y.dma_start(
                        out=w2sb[sl2][:, :], in_=w2c_h[e][dcb]), sW2r[sl2])
                    v_w2s = (sW2r[sl2], sW2r[sl2].n)
                w(qT, *v_w2s)
                yb = dcb % 2
                wT(sV, y_evac_V[yb])
                for fc in range(FC):
                    if dcb == 0:
                        wT(sV, h_gate_V[fc])
                    last = (fc == FC - 1) and (e != E - 1)
                    eT(lambda t, fc=fc, sl2=sl2, yb=yb, last=last: t.matmul(
                        psY[yb][:, :], w2sb[sl2][:, fc * 128:(fc + 1) * 128],
                        htil[:, fc * 512:(fc + 1) * 512],
                        start=(fc == 0), stop=last),
                       sem=(sT if (fc == FC - 1 and e != E - 1) else None))
                if e == E - 1:
                    eT(lambda t, dcb=dcb, qb=qb, yb=yb: t.matmul(
                        psY[yb][:, :], b2bf[0:8, dcb * 128:(dcb + 1) * 128],
                        gT[0:8, qb * 512:(qb + 1) * 512], start=False, stop=True), sT)
                w2_free_T[sl2] = sT.n
                v_ymm = sT.n
                wV(sT, v_ymm)
                eV(lambda v, dcb=dcb, qb=qb, yb=yb: v.tensor_tensor(
                    slab[:, dcb * S + qb * 512: dcb * S + (qb + 1) * 512],
                    slab[:, dcb * S + qb * 512: dcb * S + (qb + 1) * 512],
                    psY[yb][:, :], ALU.add), sV)
                y_evac_V[yb] = sV.n
                if e == E - 1 and qb > 0:
                    emit_p5_group(y_done_qb[qb - 1])
            for fc in range(FC):
                h_used_T[fc] = sT.n
            if e == E - 1:
                y_done_qb[qb] = sV.n

    while p5_cons_ptr < len(p5_groups):
        emit_p5_group(y_done_qb[NQB - 1])

    barrier()
    p4.close()
    return _finish(nc, st, taps, (qT, qV, qS, qG, qY), sO, locals())


def _finish(nc, st, taps, queues, sO, env):
    qT, qV, qS, qG, qY = queues
    tap_pairs = []
    for name in taps:
        h = env[name]
        ap = h.ap() if hasattr(h, "ap") else h
        o = nc.dram_tensor("tap_" + name, list(ap.shape), ap.dtype,
                           kind="ExternalOutput").ap()
        tap_pairs.append((o, ap))
    for so in sO:
        if so.n > 0:
            qY.append(lambda y, v=so.n, h=so.h: y.wait_ge(h, v))
    for o, ap in tap_pairs:
        sO[0].n += 16
        qY.append(lambda y, o=o, ap=ap, h=sO[0].h: y.dma_start(out=o, in_=ap).then_inc(h, 16))
        qY.append(lambda y, v=sO[0].n, h=sO[0].h: y.wait_ge(h, v))

    with nc.Block() as blk:
        @blk.tensor
        def _(t):
            for f in qT:
                f(t)

        @blk.vector
        def _(v):
            for f in qV:
                f(v)

        @blk.scalar
        def _(s):
            for f in qS:
                f(s)

        @blk.gpsimd
        def _(g):
            for f in qG:
                f(g)

        @blk.sync
        def _(y):
            for f in qY:
                f(y)
    st.close()
    return nc


def kernel(L0_attn_residual, L0_attn_W_key, L0_attn_W_query, L0_attn_W_values,
           L0_attn_W_output, L0_attn_Bias_key, L0_attn_Bias_query,
           L0_attn_Bias_values, L0_attn_Bias_output, L0_ffn_w_router,
           L0_ffn_b_router, L0_ffn_w1, L0_ffn_b1, L0_ffn_w2, L0_ffn_b2):
    nc = build_program()
    shared = {
        "wk": np.ascontiguousarray(L0_attn_W_key, np.float32),
        "wq": np.ascontiguousarray(L0_attn_W_query, np.float32),
        "wv": np.ascontiguousarray(L0_attn_W_values, np.float32),
        "wo": np.ascontiguousarray(L0_attn_W_output, np.float32),
        "bk": np.ascontiguousarray(L0_attn_Bias_key, np.float32),
        "bq": np.ascontiguousarray(L0_attn_Bias_query, np.float32),
        "bv": np.ascontiguousarray(L0_attn_Bias_values, np.float32),
        "bo": np.ascontiguousarray(L0_attn_Bias_output, np.float32),
        "wr": np.ascontiguousarray(L0_ffn_w_router, np.float32),
        "br": np.ascontiguousarray(L0_ffn_b_router, np.float32),
        "w1": np.ascontiguousarray(L0_ffn_w1, np.float32),
        "b1": np.ascontiguousarray(L0_ffn_b1, np.float32),
        "w2": np.ascontiguousarray(L0_ffn_w2, np.float32),
        "b2": np.ascontiguousarray(L0_ffn_b2, np.float32),
    }
    xs = np.ascontiguousarray(L0_attn_residual, np.float32)
    in_maps = [dict(shared, x=np.ascontiguousarray(xs[i])) for i in range(8)]
    res = run_bass_kernel_spmd(nc, in_maps, core_ids=list(range(8)))
    return np.stack([np.asarray(r["out"], np.float32) for r in res.results], axis=0)


# version 15
# speedup vs baseline: 1.1024x; 1.1024x over previous
import numpy as np
from contextlib import ExitStack

import concourse.bass as bass
import concourse.mybir as mybir
from concourse.bass_utils import run_bass_kernel_spmd

F32 = mybir.dt.float32
BF16 = mybir.dt.bfloat16
AF = mybir.ActivationFunctionType
ALU = mybir.AluOpType
AX = mybir.AxisListType

S, D, NH, HD = 4096, 1024, 16, 64
NP = NH // 2
E, F = 8, 2048
DC = D // 128
FC = F // 128
NQB = S // 512
NKC = S // 128
NTT = S // 128


class Sem:
    def __init__(self, h):
        self.h = h
        self.n = 0


def build_program(upto=5, taps=(), ntt=NTT, rtr=True):
    nc = bass.Bass("TRN2", target_bir_lowering=False)

    x_h = nc.dram_tensor("x", [S, D], F32, kind="ExternalInput").ap()
    wk_h = nc.dram_tensor("wk", [NH, D, HD], F32, kind="ExternalInput").ap()
    wq_h = nc.dram_tensor("wq", [NH, D, HD], F32, kind="ExternalInput").ap()
    wv_h = nc.dram_tensor("wv", [NH, D, HD], F32, kind="ExternalInput").ap()
    wo_h = nc.dram_tensor("wo", [NH, HD, D], F32, kind="ExternalInput").ap()
    bk_h = nc.dram_tensor("bk", [NH, HD], F32, kind="ExternalInput").ap()
    bq_h = nc.dram_tensor("bq", [NH, HD], F32, kind="ExternalInput").ap()
    bv_h = nc.dram_tensor("bv", [NH, HD], F32, kind="ExternalInput").ap()
    bo_h = nc.dram_tensor("bo", [D], F32, kind="ExternalInput").ap()
    wr_h = nc.dram_tensor("wr", [D, E], F32, kind="ExternalInput").ap()
    br_h = nc.dram_tensor("br", [E], F32, kind="ExternalInput").ap()
    w1_h = nc.dram_tensor("w1", [E, D, F], F32, kind="ExternalInput").ap()
    b1_h = nc.dram_tensor("b1", [E, F], F32, kind="ExternalInput").ap()
    w2_h = nc.dram_tensor("w2", [E, F, D], F32, kind="ExternalInput").ap()
    b2_h = nc.dram_tensor("b2", [E, D], F32, kind="ExternalInput").ap()
    out_h = nc.dram_tensor("out", [S, D], F32, kind="ExternalOutput").ap()
    w2c_h = nc.dram_tensor("w2c", [E, DC, 128, FC * 128], BF16, kind="Internal").ap()

    qT, qV, qS, qG, qY = [], [], [], [], []

    st = ExitStack()
    sT = Sem(st.enter_context(nc.semaphore("sT")))
    sV = Sem(st.enter_context(nc.semaphore("sV")))
    sS = Sem(st.enter_context(nc.semaphore("sS")))
    sG = Sem(st.enter_context(nc.semaphore("sG")))
    sIni = Sem(st.enter_context(nc.semaphore("sIni")))
    sX = [Sem(st.enter_context(nc.semaphore(f"sX{i}"))) for i in range(2)]
    sWq = [Sem(st.enter_context(nc.semaphore(f"sWq{i}"))) for i in range(2)]
    sW1 = [Sem(st.enter_context(nc.semaphore(f"sW1{i}"))) for i in range(2)]
    sW2 = [Sem(st.enter_context(nc.semaphore(f"sW2{i}"))) for i in range(3)]
    sWo = Sem(st.enter_context(nc.semaphore("sWo")))
    sW2w = [Sem(st.enter_context(nc.semaphore(f"sW2w{i}"))) for i in range(3)]
    sW2r = [Sem(st.enter_context(nc.semaphore(f"sW2r{i}"))) for i in range(3)]
    sB2 = Sem(st.enter_context(nc.semaphore("sB2")))
    sO = [Sem(st.enter_context(nc.semaphore(f"sO{i}"))) for i in range(2)]
    ALLSEMS = [sT, sV, sS, sG, sIni, sX[0], sX[1], sWq[0], sWq[1],
               sW1[0], sW1[1], sW2[0], sW2[1], sW2[2], sW2w[0], sW2w[1],
               sW2w[2], sW2r[0], sW2r[1], sW2r[2], sWo, sB2, sO[0], sO[1]]

    def _emit(q, fn, sem=None, k=1):
        if sem is None:
            q.append(fn)
            return None
        sem.n += k
        q.append(lambda e, f=fn, h=sem.h, k=k: f(e).then_inc(h, k))
        return sem.n

    def eT(fn, sem=None):  return _emit(qT, fn, sem)
    def eV(fn, sem=None):  return _emit(qV, fn, sem)
    def eS(fn, sem=None):  return _emit(qS, fn, sem)
    def eG(fn, sem=None, k=16):  return _emit(qG, fn, sem, k)
    def eY(fn, sem=None, k=16):  return _emit(qY, fn, sem, k)

    def w(q, sem, v):
        if v and v > 0:
            q.append(lambda e, h=sem.h, v=v: e.wait_ge(h, v))
    def wT(sem, v): w(qT, sem, v)
    def wV(sem, v): w(qV, sem, v)
    def wS(sem, v): w(qS, sem, v)
    def wG(sem, v): w(qG, sem, v)
    def wY(sem, v): w(qY, sem, v)

    def vflush(): wV(sV, sV.n)
    def sflush(): wS(sS, sS.n)
    def gflush(): wG(sG, sG.n)

    def barrier():
        snap = [(s.h, s.n) for s in ALLSEMS if s.n > 0]
        for q in (qT, qV, qS, qG, qY):
            for h, n in snap:
                q.append(lambda e, h=h, n=n: e.wait_ge(h, n))

    A = nc.alloc_sbuf_tensor

    xnT = A("xnT", [128, DC * S], BF16)
    slab = A("slab", [128, 8 * S], BF16)
    identF = A("identF", [128, 128], F32)
    identB = A("identB", [128, 128], BF16)

    biasfull = A("biasfull", [128, 32], F32)
    b1T = A("b1T", [128, 128], F32)

    brS = A("brS", [8, 1], F32)
    wrS = A("wrS", [128, DC * E], F32)
    gT = A("gT", [8, S], BF16)
    epsb = A("epsb", [128, 1], F32)

    p0 = ExitStack()
    scr0 = p0.enter_context(nc.sbuf_tensor("scr0", [128, 1024], F32))
    brow = p0.enter_context(nc.sbuf_tensor("brow", [32, 128], F32))
    b1row = p0.enter_context(nc.sbuf_tensor("b1row", [128, 128], F32))
    psA0 = p0.enter_context(nc.psum_tensor("psA0", [128, 128], F32))
    psB0 = p0.enter_context(nc.psum_tensor("psB0", [128, 128], F32))

    eY(lambda y: y.dma_start(out=brow[0:8, :], in_=bq_h.rearrange("(pr two) h -> pr (two h)", two=2)), sIni)
    eY(lambda y: y.dma_start(out=brow[8:16, :], in_=bk_h.rearrange("(pr two) h -> pr (two h)", two=2)), sIni)
    eY(lambda y: y.dma_start(out=brow[16:24, :], in_=bv_h.rearrange("(pr two) h -> pr (two h)", two=2)), sIni)
    eY(lambda y: y.dma_start(out=brow[24:32, :], in_=bo_h.rearrange("(ec p) -> ec p", p=128)), sIni)
    eY(lambda y: y.dma_start(out=b1row[:, :], in_=b1_h.rearrange("e (fc p) -> (e fc) p", p=128)), sIni)
    for dc in range(DC):
        eY(lambda y, dc=dc: y.dma_start(out=wrS[:, dc * 8:(dc + 1) * 8],
                                        in_=wr_h[dc * 128:(dc + 1) * 128, :]), sIni)
    eY(lambda y: y.dma_start(out=brS[0:8, 0:1], in_=br_h.rearrange("(e one) -> e one", one=1)), sIni)
    v_init_loads = sIni.n


    eG(lambda g: g.iota(scr0[:, 0:128], [[1, 128]], base=0, channel_multiplier=-1,
                        allow_small_or_imprecise_dtypes=True), sG, 1)
    eG(lambda g: g.memset(epsb[:, :], 1e-5), sG, 1)
    v_g_ident = sG.n
    v_g_all = sG.n

    wV(sG, v_g_ident)
    eV(lambda v: v.tensor_single_scalar(identF[:, :], scr0[:, 0:128], 0.0, ALU.is_equal), sV)
    vflush()
    eV(lambda v: v.tensor_copy(identB[:, :], identF[:, :]), sV)
    v_ident = sV.n
    wV(sG, v_g_all)

    wT(sIni, v_init_loads)
    wT(sV, v_ident)
    eT(lambda t: t.transpose(psA0[:, 0:32], brow[:, :], identF[0:32, 0:32]), sT)
    eT(lambda t: t.transpose(psB0[:, :], b1row[:, :], identF[:, :]), sT)
    v_btr = sT.n
    wV(sT, v_btr)
    eV(lambda v: v.tensor_copy(biasfull[:, 0:32], psA0[:, 0:32]), sV)
    eV(lambda v: v.tensor_copy(b1T[:, :], psB0[:, :]), sV)

    barrier()
    p0.close()
    if upto < 1:
        return _finish(nc, st, taps, (qT, qV, qS, qG, qY), sO, locals())

    p1 = ExitStack()
    Xb = [p1.enter_context(nc.sbuf_tensor(f"Xb{i}", [128, D], F32)) for i in range(2)]
    xnb = [p1.enter_context(nc.sbuf_tensor(f"xnb{i}", [128, D], F32)) for i in range(2)]
    ttr_scr = p1.enter_context(nc.sbuf_tensor("ttr_scr", [128, D], F32))
    win = [p1.enter_context(nc.sbuf_tensor(f"win{i}", [128, DC * 512], F32)) for i in range(2)]
    s1 = p1.enter_context(nc.sbuf_tensor("s1", [128, 1], F32))
    m2 = p1.enter_context(nc.sbuf_tensor("m2", [128, 1], F32))
    meanb = p1.enter_context(nc.sbuf_tensor("meanb", [128, 1], F32))
    msq = p1.enter_context(nc.sbuf_tensor("msq", [128, 1], F32))
    varb = p1.enter_context(nc.sbuf_tensor("varb", [128, 1], F32))
    lnv = p1.enter_context(nc.sbuf_tensor("lnv", [128, 1], F32))
    rstd = p1.enter_context(nc.sbuf_tensor("rstd", [128, 1], F32))
    lgT = p1.enter_context(nc.sbuf_tensor("lgT", [8, 512], F32))
    lgtok = p1.enter_context(nc.sbuf_tensor("lgtok", [128, 8], F32))
    mx = p1.enter_context(nc.sbuf_tensor("mx", [128, 1], F32))
    nmx = p1.enter_context(nc.sbuf_tensor("nmx", [128, 1], F32))
    exs = p1.enter_context(nc.sbuf_tensor("exs", [128, 8], F32))
    sume = p1.enter_context(nc.sbuf_tensor("sume", [128, 1], F32))
    epb = p1.enter_context(nc.sbuf_tensor("epb", [128, 1], F32))
    msks = p1.enter_context(nc.sbuf_tensor("msks", [128, 8], F32))
    gtok = p1.enter_context(nc.sbuf_tensor("gtok", [128, 8], F32))
    psTr = [p1.enter_context(nc.psum_tensor(f"psTr{i}", [128, 128], F32)) for i in range(2)]
    psR = p1.enter_context(nc.psum_tensor("psR", [8, 512], F32))
    psLg = p1.enter_context(nc.psum_tensor("psLg", [128, 8], F32))
    psG1 = p1.enter_context(nc.psum_tensor("psG1", [8, 128], F32))

    xb_free = [0, 0]
    xb_free_S = [0, 0]
    xn_last_T = [0, 0]
    tr_evac_V = [0, 0]
    win_free_T = [0, 0]
    win_free_S = [0, 0]
    psR_evac = 0
    psLg_free = 0
    psG_free = 0
    tr_idx = 0

    for i in range(ntt):
        sl = i % 2
        wsl = (i // 4) % 2
        if sl == 0:
            wY(sV, xb_free[sl])
            wY(sS, xb_free_S[sl])
            eY(lambda y, i=i, sl=sl: y.dma_start(out=Xb[sl][:, :],
                                                 in_=x_h[i * 128:(i + 1) * 128, :]), sX[sl])
        else:
            wG(sV, xb_free[sl])
            wG(sS, xb_free_S[sl])
            eG(lambda g, i=i, sl=sl: g.dma_start(out=Xb[sl][:, :],
                                                 in_=x_h[i * 128:(i + 1) * 128, :]), sX[sl])
        v_x = sX[sl].n
        wV(sX[sl], v_x)
        eV(lambda v, sl=sl: v.reduce_sum(s1[:, 0:1], Xb[sl][:, :], axis=AX.X), sV)
        wS(sX[sl], v_x)
        eS(lambda s, sl=sl: s.activation(ttr_scr[:, :], Xb[sl][:, :], AF.Square,
                                         accum_out=m2[:, 0:1]), sS)
        v_sq = sS.n
        xb_free_S[sl] = v_sq
        vflush()
        eV(lambda v: v.tensor_single_scalar(meanb[:, 0:1], s1[:, 0:1], 1.0 / D, ALU.mult), sV)
        vflush()
        eV(lambda v: v.tensor_tensor(msq[:, 0:1], meanb[:, 0:1], meanb[:, 0:1], ALU.mult), sV)
        wV(sS, v_sq)
        eV(lambda v: v.tensor_single_scalar(m2[:, 0:1], m2[:, 0:1], 1.0 / D, ALU.mult), sV)
        vflush()
        eV(lambda v: v.tensor_tensor(varb[:, 0:1], m2[:, 0:1], msq[:, 0:1], ALU.subtract), sV)
        v_var = sV.n
        wS(sV, v_var)
        eS(lambda s: s.activation(lnv[:, 0:1], varb[:, 0:1], AF.Ln, bias=epsb[:, 0:1]), sS)
        sflush()
        eS(lambda s: s.activation(rstd[:, 0:1], lnv[:, 0:1], AF.Exp, scale=-0.5), sS)
        v_rstd = sS.n
        wV(sS, v_rstd)
        wV(sT, xn_last_T[sl])
        eV(lambda v, sl=sl: v.tensor_scalar(
            xnb[sl][:, :], Xb[sl][:, :], meanb[:, 0:1], rstd[:, 0:1],
            ALU.subtract, ALU.mult), sV)
        v_xn = sV.n
        xb_free[sl] = v_xn
        for dc in range(DC):
            bk = tr_idx % 2
            tr_idx += 1
            wT(sV, v_xn)
            wT(sV, tr_evac_V[bk])
            eT(lambda t, sl=sl, dc=dc, bk=bk: t.transpose(
                psTr[bk][:, :], xnb[sl][:, dc * 128:(dc + 1) * 128], identF[:, :]), sT)
            v_tr = sT.n
            wV(sT, v_tr)
            if i % 4 == 0 and dc == 0:
                wV(sT, win_free_T[wsl])
                wV(sS, win_free_S[wsl])
            eV(lambda v, dc=dc, i=i, bk=bk, wsl=wsl: v.tensor_copy(
                win[wsl][:, dc * 512 + (i % 4) * 128: dc * 512 + (i % 4) * 128 + 128],
                psTr[bk][:, :]), sV)
            tr_evac_V[bk] = sV.n
            wS(sV, tr_evac_V[bk])
            eS(lambda s, dc=dc, i=i, wsl=wsl: s.copy(
                xnT[:, dc * S + i * 128: dc * S + (i + 1) * 128],
                win[wsl][:, dc * 512 + (i % 4) * 128: dc * 512 + (i % 4) * 128 + 128]), sS)
        xn_last_T[sl] = sT.n
        if i % 4 == 3:
            win_free_S[wsl] = sS.n

        if i % 4 == 3 and rtr:
            v_win = sV.n
            wT(sV, v_win)
            wT(sS, psR_evac)
            for dc in range(DC):
                eT(lambda t, dc=dc, wsl=wsl: t.matmul(
                    psR[0:8, :], wrS[:, dc * 8:(dc + 1) * 8],
                    win[wsl][:, dc * 512:(dc + 1) * 512],
                    start=(dc == 0), stop=(dc == DC - 1)),
                   sem=(sT if dc == DC - 1 else None))
            win_free_T[wsl] = sT.n
            v_rmm = sT.n
            wS(sT, v_rmm)
            eS(lambda s: s.activation(lgT[0:8, :], psR[0:8, :], AF.Identity,
                                      bias=brS[0:8, 0:1]), sS)
            psR_evac = sS.n
            v_lg = sS.n
            for j in range(4):
                tt = (i // 4) * 4 + j
                wT(sS, v_lg)
                wT(sV, psLg_free)
                eT(lambda t, j=j: t.transpose(psLg[:, 0:8], lgT[0:8, j * 128:(j + 1) * 128],
                                              identF[0:8, 0:8]), sT)
                v_lgt = sT.n
                wV(sT, v_lgt)
                eV(lambda v: v.tensor_copy(lgtok[:, 0:8], psLg[:, 0:8]), sV)
                psLg_free = sV.n
                vflush()
                eV(lambda v: v.reduce_max(mx[:, 0:1], lgtok[:, 0:8], axis=AX.X), sV)
                vflush()
                eV(lambda v: v.tensor_single_scalar(nmx[:, 0:1], mx[:, 0:1], -1.0, ALU.mult), sV)
                v_mx = sV.n
                wS(sV, v_mx)
                eS(lambda s: s.activation(exs[:, 0:8], lgtok[:, 0:8], AF.Exp,
                                          bias=nmx[:, 0:1], accum_out=sume[:, 0:1]), sS)
                v_se = sS.n
                wV(sS, v_se)
                eV(lambda v: v.reciprocal(epb[:, 0:1], sume[:, 0:1]), sV)
                eV(lambda v: v.tensor_single_scalar(msks[:, 0:8], lgtok[:, 0:8],
                                                    mx[:, 0:1], ALU.is_ge), sV)
                vflush()
                eV(lambda v: v.tensor_scalar(gtok[:, 0:8], msks[:, 0:8], epb[:, 0:1], None,
                                             ALU.mult), sV)
                v_g = sV.n
                wT(sV, v_g)
                wT(sV, psG_free)
                eT(lambda t: t.transpose(psG1[0:8, :], gtok[:, 0:8], identF[:, :]), sT)
                v_gt = sT.n
                wV(sT, v_gt)
                eV(lambda v, tt=tt: v.tensor_copy(gT[0:8, tt * 128:(tt + 1) * 128],
                                                  psG1[0:8, :]), sV)
                psG_free = sV.n

    barrier()
    p1.close()
    if upto < 2:
        return _finish(nc, st, taps, (qT, qV, qS, qG, qY), sO, locals())

    p2 = ExitStack()
    wqkv = [p2.enter_context(nc.sbuf_tensor("wqkv0", [128, 3 * 1024], BF16))]
    KTs = p2.enter_context(nc.sbuf_tensor("KTs", [128, S], BF16))
    QTs = p2.enter_context(nc.sbuf_tensor("QTs", [128, S], BF16))
    VTs = [p2.enter_context(nc.sbuf_tensor(f"VTs{i}", [128, 512], BF16)) for i in range(2)]
    Vp0 = p2.enter_context(nc.sbuf_tensor("Vp0", [128, NKC * 128], BF16))
    Vp1 = p2.enter_context(nc.sbuf_tensor("Vp1", [128, NKC * 128], BF16))
    Pt = [p2.enter_context(nc.sbuf_tensor(f"Pt{i}", [128, 2048], BF16)) for i in range(2)]
    sufSB = p2.enter_context(nc.sbuf_tensor("sufSB", [1, NQB * 256], BF16))
    ones512 = p2.enter_context(nc.sbuf_tensor("ones512", [128, 512], BF16))
    onescolB = p2.enter_context(nc.sbuf_tensor("onescolB", [128, 1], BF16))
    Zbuf = [p2.enter_context(nc.sbuf_tensor(f"Zbuf{i}", [96, 512], F32)) for i in range(2)]
    rzbsb = p2.enter_context(nc.sbuf_tensor("rzbsb", [128, 512], F32))
    sufacc = p2.enter_context(nc.sbuf_tensor("sufacc", [1, 256], F32))
    masks = p2.enter_context(nc.sbuf_tensor("masks", [128, 4 * 1024], BF16))
    spairB = p2.enter_context(nc.psum_tensor("spairB", [128, 2048], F32))

    for j in range(2):
        eG(lambda g, j=j: g.iota(
            masks.ap().rearrange("p (j d two f) -> p j d two f", j=2, d=2, two=2)[:, j],
            [[-128, 2], [0, 2], [1, 512]], base=-(2 * j * 128), channel_multiplier=-1,
            allow_small_or_imprecise_dtypes=True), sG, 1)
    v_g_masks = sG.n
    wV(sG, v_g_masks)
    for j in range(2):
        eV(lambda v, j=j: v.tensor_single_scalar(
            masks[:, j * 2048:(j + 1) * 2048], masks[:, j * 2048:(j + 1) * 2048],
            0.0, ALU.is_ge), sV)
    vflush()
    v_masks_done = sV.n

    eG(lambda g: g.memset(onescolB[:, :], 1.0), sG, 1)
    eG(lambda g: g.memset(ones512[:, :], 1.0), sG, 1)
    eG(lambda g: g.memset(Vp0[:, :], 0.0), sG, 1)
    gflush()
    eG(lambda g: g.memset(
        Vp0.ap().rearrange("p (kc c) -> p kc c", c=128)[:, :, 64:65], 1.0), sG, 1)
    eG(lambda g: g.memset(Vp1[:, :], 0.0), sG, 1)
    gflush()
    eG(lambda g: g.memset(
        Vp1.ap().rearrange("p (kc c) -> p kc c", c=128)[:, :, 0:1], 1.0), sG, 1)
    v_vpinit = sG.n
    wT(sG, v_vpinit)
    wV(sG, v_vpinit)

    def qkv_dma(p):
        for m, wh in ((0, wk_h), (1, wq_h), (2, wv_h)):
            for hh in range(2):
                n = 2 * p + hh
                eG(lambda g, m=m, wh=wh, n=n, hh=hh: g.dma_start(
                    out=wqkv[0].ap().rearrange("q (m dc h) -> q m dc h", m=3, dc=DC)
                        [:, m, :, hh * 64:(hh + 1) * 64],
                    in_=wh[n].rearrange("(dc q) h -> q dc h", q=128)), sWq[p % 2])
        return sWq[p % 2].n

    wq_ready = [0, 0]
    wq_free_T = [0, 0]
    ktqt_free_T = 0
    vp_free_T = 0
    spair_slot_S = [0, 0]
    spair_exp_S = 0
    pv_done = [0, 0]
    o_free_V = 0
    o_free_S = 0
    norm_done_V = 0
    zbuf_free_V = [0, 0]
    zsl = 0
    wq_ready[0] = qkv_dma(0)

    for p in range(NP):
        pA = ExitStack()
        psVt = [pA.enter_context(nc.psum_tensor(f"psVt{p}_{i}", [128, 128], BF16)) for i in range(2)]
        wT(sWq[p % 2], wq_ready[p % 2])
        wS(sT, ktqt_free_T)
        wV(sT, vp_free_T)
        wT(sV, o_free_V)
        wT(sS, o_free_S)
        wS(sV, o_free_V)
        bidx = 0
        vts_free_T = [0, 0]
        vtr_free_V = [0, 0]
        for m, dst, bcol in ((0, KTs, 8), (1, QTs, 0), (2, None, 16)):
            for kb in range(NQB):
                bk = bidx % 2
                bidx += 1
                wT(sS, spair_slot_S[bk])
                for dc in range(DC):
                    eT(lambda t, p=p, m=m, dc=dc, kb=kb, bk=bk: t.matmul(
                        spairB[:, bk * 1024: bk * 1024 + 512],
                        wqkv[0][:, m * 1024 + dc * 128: m * 1024 + (dc + 1) * 128],
                        xnT[:, dc * S + kb * 512: dc * S + (kb + 1) * 512],
                        start=(dc == 0), stop=(dc == DC - 1)),
                       sem=(sT if dc == DC - 1 else None))
                v_mm = sT.n
                wS(sT, v_mm)
                if m < 2:
                    eS(lambda s, dst=dst, kb=kb, bk=bk, bcol=bcol, p=p: s.activation(
                        dst[:, kb * 512:(kb + 1) * 512],
                        spairB[:, bk * 1024: bk * 1024 + 512], AF.Identity,
                        bias=biasfull[:, bcol + p: bcol + p + 1]), sS)
                    spair_slot_S[bk] = sS.n
                else:
                    vsl = kb % 2
                    wS(sT, vts_free_T[vsl])
                    eS(lambda s, vsl=vsl, bk=bk, bcol=bcol, p=p: s.activation(
                        VTs[vsl][:, :], spairB[:, bk * 1024: bk * 1024 + 512], AF.Identity,
                        bias=biasfull[:, bcol + p: bcol + p + 1]), sS)
                    spair_slot_S[bk] = sS.n
                    v_vts = sS.n
                    for j in range(4):
                        kc = kb * 4 + j
                        tb = kc % 2
                        wT(sS, v_vts)
                        wT(sV, vtr_free_V[tb])
                        eT(lambda t, j=j, tb=tb, vsl=vsl: t.transpose(
                            psVt[tb][:, :], VTs[vsl][:, j * 128:(j + 1) * 128],
                            identB[:, :]), sT)
                        v_tr = sT.n
                        wV(sT, v_tr)
                        eV(lambda v, kc=kc, tb=tb: v.tensor_copy(
                            Vp0[:, kc * 128: kc * 128 + 64], psVt[tb][:, 0:64]), sV)
                        eV(lambda v, kc=kc, tb=tb: v.tensor_copy(
                            Vp1[:, kc * 128 + 64: kc * 128 + 128], psVt[tb][:, 64:128]), sV)
                        vtr_free_V[tb] = sV.n
                    vts_free_T[vsl] = sT.n
        wq_free_T[p % 2] = sT.n
        v_vp_done = sV.n
        pA.close()
        if p + 1 < NP:
            wG(sT, wq_free_T[p % 2])
            wq_ready[(p + 1) % 2] = qkv_dma(p + 1)

        pB = ExitStack()
        psO0 = pB.enter_context(nc.psum_tensor(f"psO0_{p}", [128, 512], F32))
        psO1 = pB.enter_context(nc.psum_tensor(f"psO1_{p}", [128, 512], F32))
        psSuf = pB.enter_context(nc.psum_tensor(f"psSuf_{p}", [1, 256], F32))
        pending = []

        def emit_norm():
            nonlocal norm_done_V
            ep, eqb, ezsl, zready = pending.pop(0)
            wV(sS, zready[1])
            for dst0, src0 in ((0, 64), (32, 64), (64, 0), (96, 0)):
                eV(lambda v, ezsl=ezsl, dst0=dst0, src0=src0: v.stream_shuffle(
                    rzbsb[dst0:dst0 + 32, :], Zbuf[ezsl][src0:src0 + 32, :],
                    [0] * 32), sV)
            zbuf_free_V[ezsl] = sV.n
            vflush()
            eV(lambda v: v.reciprocal(rzbsb[:, :], rzbsb[:, :]), sV)
            vflush()
            eV(lambda v, ep=ep, eqb=eqb: v.tensor_tensor(
                slab[:, ep * S + eqb * 512: ep * S + (eqb + 1) * 512],
                slab[:, ep * S + eqb * 512: ep * S + (eqb + 1) * 512],
                rzbsb[:, :], ALU.mult), sV)
            norm_done_V = sV.n

        wT(sV, v_vp_done)
        suf_evac_V = 0
        suf_ready = {}
        suf_steps = list(range(NQB - 2, -1, -1))

        def emit_suf_step():
            nonlocal suf_evac_V
            sqb = suf_steps.pop(0)
            wT(sV, suf_evac_V)
            cs = list(range(4 * sqb + 4, 4 * sqb + 8))
            for ci, c in enumerate(cs):
                eT(lambda t, c=c, s0=(ci == 0): t.matmul(
                    psSuf[0:1, 0:128], onescolB[:, 0:1], Vp0[:, c * 128:(c + 1) * 128],
                    start=s0, stop=False))
                eT(lambda t, c=c, e0=(ci == len(cs) - 1): t.matmul(
                    psSuf[0:1, 128:256], onescolB[:, 0:1], Vp1[:, c * 128:(c + 1) * 128],
                    start=False, stop=e0), sT)
            v_sufmm = sT.n
            wV(sT, v_sufmm)
            if sqb == NQB - 2:
                eV(lambda v: v.tensor_copy(sufacc[0:1, :], psSuf[0:1, 0:256]), sV)
            else:
                eV(lambda v: v.tensor_tensor(
                    sufacc[0:1, :], psSuf[0:1, 0:256], sufacc[0:1, :], ALU.add), sV)
            vflush()
            eV(lambda v, sqb=sqb: v.tensor_copy(
                sufSB[0:1, sqb * 256:(sqb + 1) * 256], sufacc[0:1, :]), sV)
            suf_evac_V = sV.n
            suf_ready[sqb] = sV.n

        rc = 0
        for qb in range(NQB - 1, -1, -1):
            reals = list(range(0, 4 * qb + 4))
            ofirst = True

            def emit_smm(idx):
                nonlocal spair_exp_S
                c = reals[idx]
                sl = (rc0 + idx) % 2
                wT(sS, spair_slot_S[sl])
                eT(lambda t, c=c, sl=sl, qb=qb: t.matmul(
                    spairB[:, sl * 1024: sl * 1024 + 512],
                    KTs[0:64, c * 128:(c + 1) * 128],
                    QTs[0:64, qb * 512:(qb + 1) * 512], start=True, stop=True))
                eT(lambda t, c=c, sl=sl, qb=qb: t.matmul(
                    spairB[:, sl * 1024 + 512: sl * 1024 + 1024],
                    KTs[64:128, c * 128:(c + 1) * 128],
                    QTs[64:128, qb * 512:(qb + 1) * 512], start=True, stop=True), sT)
                v_smm = sT.n
                if c >= 4 * qb:
                    d = c - 4 * qb
                    moff = (d // 2) * 2048 + (d % 2) * 1024
                    wV(sT, v_smm)
                    eV(lambda v, sl=sl, moff=moff: v.tensor_tensor(
                        spairB[:, sl * 1024:(sl + 1) * 1024],
                        spairB[:, sl * 1024:(sl + 1) * 1024],
                        masks[:, moff: moff + 1024], ALU.mult), sV)
                    wS(sV, sV.n)
                else:
                    wS(sT, v_smm)
                wS(sT, pv_done[sl])
                eS(lambda s, sl=sl: s.activation(
                    Pt[sl][:, 0:1024], spairB[:, sl * 1024:(sl + 1) * 1024],
                    AF.Exp, scale=0.125), sS)
                spair_slot_S[sl] = sS.n
                spair_exp_S = sS.n

            def emit_pv(idx):
                c = reals[idx]
                sl = (rc0 + idx) % 2
                wT(sS, spair_slot_S[sl])
                first = ofirst and idx == 0
                last = idx == len(reals) - 1
                eT(lambda t, c=c, sl=sl, first=first, last=last: t.matmul(
                    psO0[:, :], Vp0[:, c * 128:(c + 1) * 128],
                    Pt[sl][:, 0:512], start=first, stop=last))
                eT(lambda t, c=c, sl=sl, first=first, last=last: t.matmul(
                    psO1[:, :], Vp1[:, c * 128:(c + 1) * 128],
                    Pt[sl][:, 512:1024], start=first, stop=last), sT)
                pv_done[sl] = sT.n

            rc0 = rc
            emit_smm(0)
            if len(reals) > 1:
                emit_smm(1)
            wT(sV, o_free_V)
            wT(sS, o_free_S)
            if qb < NQB - 1:
                wT(sV, suf_ready[qb])
                eT(lambda t, qb=qb: t.matmul(
                    psO0[:, :], sufSB[0:1, qb * 256: qb * 256 + 128], ones512[0:1, :],
                    start=True, stop=False))
                eT(lambda t, qb=qb: t.matmul(
                    psO1[:, :], sufSB[0:1, qb * 256 + 128: qb * 256 + 256], ones512[0:1, :],
                    start=True, stop=False), sT)
                ofirst = False
            for idx in range(len(reals)):
                emit_pv(idx)
                if qb == NQB - 1 and suf_steps and idx % 4 == 1:
                    emit_suf_step()
                if idx + 2 < len(reals):
                    emit_smm(idx + 2)
            rc += len(reals)
            pv_last = sT.n
            wV(sT, pv_last)
            wS(sT, pv_last)
            wV(sV, zbuf_free_V[zsl])
            wS(sV, zbuf_free_V[zsl])
            eV(lambda v, p=p, qb=qb: v.tensor_copy(
                slab[0:64, p * S + qb * 512: p * S + (qb + 1) * 512], psO0[0:64, :]), sV)
            eS(lambda s, zsl=zsl: s.copy(Zbuf[zsl][0:1, :], psO1[0:1, :]), sS)
            v_ph1_v = sV.n
            v_ph1_s = sS.n
            wV(sS, v_ph1_s)
            wS(sV, v_ph1_v)
            eV(lambda v, p=p, qb=qb: v.tensor_copy(
                slab[64:128, p * S + qb * 512: p * S + (qb + 1) * 512],
                psO1[64:128, :]), sV)
            eS(lambda s, zsl=zsl: s.copy(Zbuf[zsl][64:65, :], psO0[64:65, :]), sS)
            o_free_V = sV.n
            o_free_S = sS.n
            zb_ready = (sV.n, sS.n)
            pending.append((p, qb, zsl, zb_ready))
            zsl ^= 1
            if len(pending) > 1:
                emit_norm()
        while pending:
            emit_norm()
        pB.close()
        ktqt_free_T = sT.n
        vp_free_T = sT.n

    barrier()
    p2.close()
    if upto < 3:
        return _finish(nc, st, taps, (qT, qV, qS, qG, qY), sO, locals())

    p3 = ExitStack()
    woT = p3.enter_context(nc.sbuf_tensor("woT", [128, NP * 1024], BF16))
    psP3 = [p3.enter_context(nc.psum_tensor(f"psP3{i}", [128, 512], F32)) for i in range(8)]
    for p in range(NP):
        eG(lambda g, p=p: g.dma_start(
            out=woT[:, p * 1024:(p + 1) * 1024],
            in_=wo_h[2 * p:2 * p + 2].rearrange("two h e -> (two h) e")), sWo)
    wT(sWo, sWo.n)
    p3_evac = [0] * 8
    for qb in range(NQB):
        for ec in range(8):
            wT(sV, p3_evac[ec])
            for p in range(NP):
                eT(lambda t, ec=ec, p=p, qb=qb: t.matmul(
                    psP3[ec][:, :], woT[:, p * 1024 + ec * 128: p * 1024 + (ec + 1) * 128],
                    slab[:, p * S + qb * 512: p * S + (qb + 1) * 512],
                    start=(p == 0), stop=(p == NP - 1)),
                   sem=(sT if p == NP - 1 else None))
        v_all = sT.n
        for ec in range(8):
            wV(sT, v_all)
            eV(lambda v, ec=ec, qb=qb: v.tensor_scalar(
                slab[:, ec * S + qb * 512: ec * S + (qb + 1) * 512],
                psP3[ec][:, :], biasfull[:, 24 + ec: 25 + ec], None, ALU.add), sV)
            p3_evac[ec] = sV.n

    barrier()
    p3.close()
    if upto < 4:
        return _finish(nc, st, taps, (qT, qV, qS, qG, qY), sO, locals())

    p4 = ExitStack()
    w1sb = p4.enter_context(nc.sbuf_tensor("w1sb", [128, DC * F], BF16))
    oneh = p4.enter_context(nc.sbuf_tensor("oneh", [8, E * 128], BF16))
    b2bf = p4.enter_context(nc.sbuf_tensor("b2bf", [8, D], BF16))
    w2sb = [p4.enter_context(nc.sbuf_tensor(f"w2sb{i}", [128, FC * 128], BF16)) for i in range(3)]
    htil = p4.enter_context(nc.sbuf_tensor("htil", [128, FC * 512], BF16))
    gb = [p4.enter_context(nc.sbuf_tensor(f"gb{i}", [128, 512], BF16)) for i in range(2)]
    psH = [p4.enter_context(nc.psum_tensor(f"psH{i}", [128, 512], F32)) for i in range(2)]
    psY = [p4.enter_context(nc.psum_tensor(f"psY{i}", [128, 512], F32)) for i in range(2)]
    psB4 = p4.enter_context(nc.psum_tensor("psB4", [128, 512], F32))
    Xb5 = [p4.enter_context(nc.sbuf_tensor(f"Xb5{i}", [128, 512], F32)) for i in range(2)]
    psF = [p4.enter_context(nc.psum_tensor(f"psF{i}", [128, 512], BF16)) for i in range(2)]

    w1_ready = [0, 0]
    w1_free_T = [0, 0]
    w2_free_T = [0, 0, 0]
    h_evac_S = [0, 0]
    h_gate_V = [0] * FC
    h_used_T = [0] * FC
    y_evac_V = [0, 0]
    gb_free_V = [0, 0]
    psB_evac_S = 0
    w2_wb = [0, 0, 0]

    def w1_dma(e, half):
        f0 = half * 1024
        wG(sT, w1_free_T[half])
        eG(lambda g, e=e, f0=f0: g.dma_start(
            out=w1sb.ap().rearrange("q (dc f) -> q dc f", dc=DC)[:, :, f0:f0 + 1024],
            in_=w1_h[e].rearrange("(dc q) f -> q dc f", q=128)[:, :, f0:f0 + 1024]),
            sW1[half])
        w1_ready[half] = sW1[half].n

    eG(lambda g: g.dma_start(out=b2bf[0:8, :], in_=b2_h[:, :]), sB2)
    eG(lambda g: g.iota(oneh.ap().rearrange("p (e c) -> p e c", e=E),
                        [[0, E], [0, 128]], base=0, channel_multiplier=1,
                        allow_small_or_imprecise_dtypes=True), sG, 1)
    v_onehramp = sG.n
    wV(sG, v_onehramp)
    for e in range(E):
        eV(lambda v, e=e: v.tensor_single_scalar(
            oneh[0:8, e * 128:(e + 1) * 128], oneh[0:8, e * 128:(e + 1) * 128],
            float(e), ALU.is_equal), sV)
    vflush()
    v_oneh_done = sV.n
    wT(sV, v_oneh_done)
    wT(sB2, sB2.n)
    w1_dma(0, 0)
    w1_dma(0, 1)

    x5_free = [0, 0]
    x5_out = [0, 0]
    x5_ready = {}
    psF_evac = [0, 0]
    p5_gidx = 0
    y_done_qb = [0] * NQB

    p5_groups = [(4 * qq + k // 2, k % 2) for qq in range(NQB) for k in range(8)]
    p5_load_ptr = 0
    p5_cons_ptr = 0

    def emit_p5_load_next():
        nonlocal p5_load_ptr
        if p5_load_ptr >= len(p5_groups):
            return
        i, g2 = p5_groups[p5_load_ptr]
        sl = p5_load_ptr % 2
        p5_load_ptr += 1
        wG(sV, x5_free[sl])
        wG(sO[sl], x5_out[sl])
        eG(lambda g, i=i, g2=g2, sl=sl: g.dma_start(
            out=Xb5[sl][:, :],
            in_=x_h[i * 128:(i + 1) * 128, g2 * 512:(g2 + 1) * 512]), sX[sl])
        x5_ready[(i, g2)] = sX[sl].n

    def emit_p5_group(yv):
        nonlocal p5_cons_ptr, p5_gidx
        i, g2 = p5_groups[p5_cons_ptr]
        sl = p5_cons_ptr % 2
        p5_cons_ptr += 1
        bk = p5_gidx % 2
        p5_gidx += 1
        wV(sX[sl], x5_ready[(i, g2)])
        wT(sV, psF_evac[bk])
        wT(sV, yv)
        for j in range(4):
            dc = g2 * 4 + j
            eT(lambda t, i=i, dc=dc, bk=bk, j=j: t.transpose(
                psF[bk][:, j * 128:(j + 1) * 128],
                slab[:, dc * S + i * 128: dc * S + (i + 1) * 128],
                identB[:, :]), sem=(sT if j == 3 else None))
        v_tr = sT.n
        wV(sT, v_tr)
        eV(lambda v, sl=sl, bk=bk: v.tensor_tensor(
            Xb5[sl][:, :], psF[bk][:, :], Xb5[sl][:, :], ALU.add), sV)
        psF_evac[bk] = sV.n
        x5_free[sl] = sV.n
        wG(sV, sV.n)
        eG(lambda g, i=i, g2=g2, sl=sl: g.dma_start(
            out=out_h[i * 128:(i + 1) * 128, g2 * 512:(g2 + 1) * 512],
            in_=Xb5[sl][:, :]), sO[sl])
        x5_out[sl] = sO[sl].n
        emit_p5_load_next()

    for e in range(E):
        for qb in range(NQB):
            if e == E - 1 and qb == 0:
                emit_p5_load_next()
                emit_p5_load_next()
            gsl = qb % 2
            wT(sV, gb_free_V[gsl])
            wT(sS, psB_evac_S)
            eT(lambda t, e=e, qb=qb: t.matmul(
                psB4[:, :], oneh[0:8, e * 128:(e + 1) * 128],
                gT[0:8, qb * 512:(qb + 1) * 512], start=True, stop=True), sT)
            v_gbm = sT.n
            wS(sT, v_gbm)
            eS(lambda s, gsl=gsl: s.copy(gb[gsl][:, :], psB4[:, :]), sS)
            psB_evac_S = sS.n
            gb_ready = sS.n
            for fc in range(FC):
                bk = fc % 2
                half = fc // 8
                if fc % 8 == 0:
                    wT(sW1[half], w1_ready[half])
                wT(sS, h_evac_S[bk])
                for dc in range(DC):
                    eT(lambda t, fc=fc, dc=dc, qb=qb, bk=bk: t.matmul(
                        psH[bk][:, :],
                        w1sb[:, dc * F + fc * 128: dc * F + (fc + 1) * 128],
                        xnT[:, dc * S + qb * 512: dc * S + (qb + 1) * 512],
                        start=(dc == 0), stop=(dc == DC - 1)),
                       sem=(sT if dc == DC - 1 else None))
                if qb == NQB - 1 and fc % 8 == 7:
                    w1_free_T[half] = sT.n
                v_hmm = sT.n
                wS(sT, v_hmm)
                wS(sT, h_used_T[fc])
                eS(lambda s, e=e, fc=fc, bk=bk: s.activation(
                    htil[:, fc * 512:(fc + 1) * 512], psH[bk][:, :], AF.Relu,
                    bias=b1T[:, e * 16 + fc: e * 16 + fc + 1]), sS)
                h_evac_S[bk] = sS.n
                v_relu = sS.n
                wV(sS, v_relu)
                wV(sS, gb_ready)
                eV(lambda v, fc=fc, gsl=gsl: v.tensor_tensor(
                    htil[:, fc * 512:(fc + 1) * 512], htil[:, fc * 512:(fc + 1) * 512],
                    gb[gsl][:, :], ALU.mult), sV)
                h_gate_V[fc] = sV.n
                if qb == NQB - 1 and e + 1 < E and fc % 8 == 7:
                    w1_dma(e + 1, fc // 8)
            gb_free_V[gsl] = sV.n
            for dcb in range(DC):
                sl2 = dcb % 3
                if qb == 0:
                    wG(sT, w2_free_T[sl2])
                    wG(sW2w[sl2], w2_wb[sl2])
                    eG(lambda g, e=e, dcb=dcb, sl2=sl2: g.dma_start(
                        out=w2sb[sl2].ap().rearrange("q (fc d) -> q fc d", fc=FC),
                        in_=w2_h[e].rearrange("(fc q) d -> q fc d", q=128)
                            [:, :, dcb * 128:(dcb + 1) * 128]), sW2[sl2])
                    v_w2 = sW2[sl2].n
                    wY(sW2[sl2], v_w2)

                    eY(lambda y, e=e, dcb=dcb, sl2=sl2: y.dma_start(
                        out=w2c_h[e][dcb], in_=w2sb[sl2][:, :]), sW2w[sl2])
                    w2_wb[sl2] = sW2w[sl2].n
                    v_w2s = (sW2[sl2], v_w2)
                else:
                    wY(sT, w2_free_T[sl2])
                    wY(sW2w[sl2], w2_wb[sl2])
                    eY(lambda y, e=e, dcb=dcb, sl2=sl2: y.dma_start(
                        out=w2sb[sl2][:, :], in_=w2c_h[e][dcb]), sW2r[sl2])
                    v_w2s = (sW2r[sl2], sW2r[sl2].n)
                w(qT, *v_w2s)
                yb = dcb % 2
                wT(sV, y_evac_V[yb])
                for fc in range(FC):
                    if dcb == 0:
                        wT(sV, h_gate_V[fc])
                    last = (fc == FC - 1) and (e != E - 1)
                    eT(lambda t, fc=fc, sl2=sl2, yb=yb, last=last: t.matmul(
                        psY[yb][:, :], w2sb[sl2][:, fc * 128:(fc + 1) * 128],
                        htil[:, fc * 512:(fc + 1) * 512],
                        start=(fc == 0), stop=last),
                       sem=(sT if (fc == FC - 1 and e != E - 1) else None))
                if e == E - 1:
                    eT(lambda t, dcb=dcb, qb=qb, yb=yb: t.matmul(
                        psY[yb][:, :], b2bf[0:8, dcb * 128:(dcb + 1) * 128],
                        gT[0:8, qb * 512:(qb + 1) * 512], start=False, stop=True), sT)
                w2_free_T[sl2] = sT.n
                v_ymm = sT.n
                wV(sT, v_ymm)
                eV(lambda v, dcb=dcb, qb=qb, yb=yb: v.tensor_tensor(
                    slab[:, dcb * S + qb * 512: dcb * S + (qb + 1) * 512],
                    slab[:, dcb * S + qb * 512: dcb * S + (qb + 1) * 512],
                    psY[yb][:, :], ALU.add), sV)
                y_evac_V[yb] = sV.n
                if e == E - 1 and qb > 0:
                    emit_p5_group(y_done_qb[qb - 1])
            for fc in range(FC):
                h_used_T[fc] = sT.n
            if e == E - 1:
                y_done_qb[qb] = sV.n

    while p5_cons_ptr < len(p5_groups):
        emit_p5_group(y_done_qb[NQB - 1])

    barrier()
    p4.close()
    return _finish(nc, st, taps, (qT, qV, qS, qG, qY), sO, locals())


def _finish(nc, st, taps, queues, sO, env):
    qT, qV, qS, qG, qY = queues
    tap_pairs = []
    for name in taps:
        h = env[name]
        ap = h.ap() if hasattr(h, "ap") else h
        o = nc.dram_tensor("tap_" + name, list(ap.shape), ap.dtype,
                           kind="ExternalOutput").ap()
        tap_pairs.append((o, ap))
    for so in sO:
        if so.n > 0:
            qY.append(lambda y, v=so.n, h=so.h: y.wait_ge(h, v))
    for o, ap in tap_pairs:
        sO[0].n += 16
        qY.append(lambda y, o=o, ap=ap, h=sO[0].h: y.dma_start(out=o, in_=ap).then_inc(h, 16))
        qY.append(lambda y, v=sO[0].n, h=sO[0].h: y.wait_ge(h, v))

    with nc.Block() as blk:
        @blk.tensor
        def _(t):
            for f in qT:
                f(t)

        @blk.vector
        def _(v):
            for f in qV:
                f(v)

        @blk.scalar
        def _(s):
            for f in qS:
                f(s)

        @blk.gpsimd
        def _(g):
            for f in qG:
                f(g)

        @blk.sync
        def _(y):
            for f in qY:
                f(y)
    st.close()
    return nc


def kernel(L0_attn_residual, L0_attn_W_key, L0_attn_W_query, L0_attn_W_values,
           L0_attn_W_output, L0_attn_Bias_key, L0_attn_Bias_query,
           L0_attn_Bias_values, L0_attn_Bias_output, L0_ffn_w_router,
           L0_ffn_b_router, L0_ffn_w1, L0_ffn_b1, L0_ffn_w2, L0_ffn_b2):
    nc = build_program()
    shared = {
        "wk": np.ascontiguousarray(L0_attn_W_key, np.float32),
        "wq": np.ascontiguousarray(L0_attn_W_query, np.float32),
        "wv": np.ascontiguousarray(L0_attn_W_values, np.float32),
        "wo": np.ascontiguousarray(L0_attn_W_output, np.float32),
        "bk": np.ascontiguousarray(L0_attn_Bias_key, np.float32),
        "bq": np.ascontiguousarray(L0_attn_Bias_query, np.float32),
        "bv": np.ascontiguousarray(L0_attn_Bias_values, np.float32),
        "bo": np.ascontiguousarray(L0_attn_Bias_output, np.float32),
        "wr": np.ascontiguousarray(L0_ffn_w_router, np.float32),
        "br": np.ascontiguousarray(L0_ffn_b_router, np.float32),
        "w1": np.ascontiguousarray(L0_ffn_w1, np.float32),
        "b1": np.ascontiguousarray(L0_ffn_b1, np.float32),
        "w2": np.ascontiguousarray(L0_ffn_w2, np.float32),
        "b2": np.ascontiguousarray(L0_ffn_b2, np.float32),
    }
    xs = np.ascontiguousarray(L0_attn_residual, np.float32)
    in_maps = [dict(shared, x=np.ascontiguousarray(xs[i])) for i in range(8)]
    res = run_bass_kernel_spmd(nc, in_maps, core_ids=list(range(8)))
    return np.stack([np.asarray(r["out"], np.float32) for r in res.results], axis=0)
